# version 23
# baseline (speedup 1.0000x reference)
import numpy as np
import concourse.bass as bass
import concourse.mybir as mybir
from concourse.bass_utils import run_bass_kernel_spmd
from contextlib import ExitStack

F32 = mybir.dt.float32
BF16 = mybir.dt.bfloat16
Alu = mybir.AluOpType
Act = mybir.ActivationFunctionType
AX = mybir.AxisListType.X

B, S, H, D, I = 4, 2048, 2048, 128, 8192
T = 1024
HC = H // 128
CH = T // 128
NTT = T // 128
ISLAB = 512
NS = I // ISLAB
NQT = ISLAB // 128
EPS_ATT = 1e-6
EPS_LN = 1e-5
CMAG_A = 1572864.0
CMAG_W = 6291456.0
CLAMP_A = 7.625
CLAMP_W = 6.5
C64 = 6755399441055744.0
K29 = 536870912.0
SEARCH_PRE = 16777216.0
SEARCH_STEPS = [16, 8, 4, 2, 1]
SEARCH_POST_S = 2.0 ** (-26)
SEARCH_POST_I = 2.0 ** 26
SKEW = True


class Prog:
    ENGS = ("sp", "pe", "dve", "act", "gp")

    def __init__(self, nc, stack):
        self.nc = nc
        self.ops = {e: [] for e in self.ENGS}
        self.semh = {}
        self.semv = {}
        self.last_w = {}
        self.readers = {}
        self.marks = {e: {} for e in self.ENGS}
        self.stack = stack
        for e in self.ENGS:
            self.mksem("e_" + e)

    def mksem(self, name):
        self.semh[name] = self.stack.enter_context(self.nc.semaphore(name))
        self.semv[name] = 0
        return name

    def emit(self, eng, build, reads=(), writes=(), sem=None, inc=1, quiet=False):
        sem = sem or ("e_" + eng)
        own = "e_" + eng
        pe_skip = (eng == "pe")
        need = {}
        for k in reads:
            tok = self.last_w.get(k)
            if tok is not None and not (pe_skip and tok[0] == own):
                need[tok[0]] = max(need.get(tok[0], 0), tok[1])
        for k in writes:
            tok = self.last_w.get(k)
            if tok is not None and not (pe_skip and tok[0] == own):
                need[tok[0]] = max(need.get(tok[0], 0), tok[1])
            for s2, v2 in self.readers.get(k, {}).items():
                if pe_skip and s2 == own:
                    continue
                need[s2] = max(need.get(s2, 0), v2)
        marks = self.marks[eng]
        waits = []
        for s2 in sorted(need):
            v2 = need[s2]
            if s2.startswith("d_"):
                v2 = self.semv[s2]
            if marks.get(s2, 0) < v2:
                waits.append((self.semh[s2], v2))
                marks[s2] = v2
        if quiet:
            tok = (sem, self.semv[sem] + inc)
            hsem = None
        else:
            self.semv[sem] += inc
            tok = (sem, self.semv[sem])
            hsem = self.semh[sem]

        def go(h, _w=tuple(waits), _b=build, _hs=hsem, _inc=inc):
            for hs, v in _w:
                h.wait_ge(hs, v)
            ins = _b(h)
            if _hs is not None:
                ins.then_inc(_hs, _inc)
        self.ops[eng].append(go)
        for k in reads:
            d = self.readers.setdefault(k, {})
            d[tok[0]] = max(d.get(tok[0], 0), tok[1])
        for k in writes:
            self.last_w[k] = tok
            self.readers[k] = {}
        return tok

    def wait_raw(self, eng, sem, val):
        hs = self.semh[sem]
        self.ops[eng].append(lambda h, _hs=hs, _v=val: h.wait_ge(_hs, _v))

    def barrier(self):
        for e in self.ENGS:
            marks = self.marks[e]
            for s, v in self.semv.items():
                if v > 0 and marks.get(s, 0) < v:
                    self.wait_raw(e, s, v)
                    marks[s] = v

    def dma(self, eng, sem, out, in_, reads=(), writes=(), transpose=False):
        return self.emit(eng, lambda h: h.dma_start(out=out, in_=in_, transpose=transpose),
                         reads=reads, writes=writes, sem=sem, inc=16)

    def tt(self, out, a, b, op, reads, writes, eng="dve"):
        return self.emit(eng, lambda h: h.tensor_tensor(out, a, b, op), reads, writes)

    def ts(self, out, a, s1, s2, op1, op2, reads, writes, eng="dve"):
        if s2 is None:
            return self.emit(eng, lambda h: h.tensor_single_scalar(out, a, s1, op1), reads, writes)
        return self.emit(eng, lambda h: h.tensor_scalar(out, a, s1, s2, op1, op2), reads, writes)

    def stt(self, out, a, s, b, op1, op2, reads, writes, eng="dve"):
        return self.emit(eng, lambda h: h.scalar_tensor_tensor(out, a, s, b, op1, op2), reads, writes)

    def cp(self, out, a, reads, writes, eng="dve"):
        return self.emit(eng, lambda h: h.tensor_copy(out, a), reads, writes)

    def recip(self, out, a, reads, writes):
        return self.emit("dve", lambda h: h.reciprocal(out, a), reads, writes)

    def red(self, out, a, op, reads, writes, absval=False):
        return self.emit("dve", lambda h: h.tensor_reduce(out, a, AX, op,
                                                          apply_absolute_value=absval),
                         reads, writes)

    def act(self, out, a, func, reads, writes, bias=0.0, scale=1.0):
        return self.emit("act", lambda h: h.activation(out, a, func, bias=bias, scale=scale),
                         reads, writes)

    def mm(self, out, lhsT, rhs, reads, writes, start=True, stop=True, quiet=False):
        return self.emit("pe", lambda h: h.matmul(out, lhsT, rhs, start=start, stop=stop),
                         reads, writes, quiet=quiet)

    def trans(self, out, in_, ident, reads, writes):
        return self.emit("pe", lambda h: h.transpose(out, in_, ident), reads, writes)

    def memset(self, ap, val, writes, eng="dve"):
        return self.emit(eng, lambda h: h.memset(ap, val), (), writes)


def r3(ap, w=32):
    return ap.rearrange("p (b w) -> p b w", w=w)


def build_program():
    nc = bass.Bass("TRN2", target_bir_lowering=False)
    xT = nc.dram_tensor("xT", [H, T], F32, kind="ExternalInput").ap()
    xP = nc.dram_tensor("xP", [H, T], F32, kind="ExternalInput").ap()
    wqT = nc.dram_tensor("wqT", [H, D], F32, kind="ExternalInput").ap()
    wkT = nc.dram_tensor("wkT", [H, D], F32, kind="ExternalInput").ap()
    wvT = nc.dram_tensor("wvT", [H, D], F32, kind="ExternalInput").ap()
    woT = nc.dram_tensor("woT", [D, H], F32, kind="ExternalInput").ap()
    Mm = nc.dram_tensor("Mm", [D, D], F32, kind="ExternalInput").ap()
    zv = nc.dram_tensor("zv", [D, 1], F32, kind="ExternalInput").ap()
    gate = nc.dram_tensor("gate", [128, 1], F32, kind="ExternalInput").ap()
    cflag = nc.dram_tensor("cflag", [128, 1], F32, kind="ExternalInput").ap()
    umask = nc.dram_tensor("umask", [128, 128], F32, kind="ExternalInput").ap()
    identf = nc.dram_tensor("identf", [128, 128], F32, kind="ExternalInput").ap()
    onescol = nc.dram_tensor("onescol", [128, 1], F32, kind="ExternalInput").ap()
    onesrow = nc.dram_tensor("onesrow", [1, 128], F32, kind="ExternalInput").ap()
    wg = nc.dram_tensor("wg", [I, H], F32, kind="ExternalInput").ap()
    wu = nc.dram_tensor("wu", [I, H], F32, kind="ExternalInput").ap()
    wd = nc.dram_tensor("wd", [H, I], F32, kind="ExternalInput").ap()
    yout = nc.dram_tensor("y", [T, H], F32, kind="ExternalOutput").ap()

    with ExitStack() as stG:
        p = Prog(nc, stG)
        sbg = lambda n, shp, dt=F32: stG.enter_context(nc.sbuf_tensor(n, shp, dt))
        BIG2 = sbg("BIG2", [128, NTT * H])
        smalls = sbg("smalls", [128, 12])

        stat = sbg("stat", [128, 16])

        for s in ["d_x", "d_xp", "d_wat", "d_small", "d_small2", "d_trg", "d_tru", "d_trd", "d_out",
                  "d_ws0", "d_ws1", "d_ws0x", "d_ws1x", "d_intq", "d_xq"]:
            p.mksem(s)

        p.dma("sp", "d_small", smalls[:, 1:2], gate, writes=["sm_gate"])
        p.dma("sp", "d_small", smalls[:, 2:3], cflag, writes=["sm_cflag"])
        p.memset(smalls[:, 4:5], 1.0, ["sm_s4"])
        p.ts(smalls[:, 4:5], smalls[:, 4:5], C64, -C64, Alu.add, Alu.add, ["sm_s4"], ["sm_s4"])
        p.ts(smalls[:, 4:5], smalls[:, 4:5], 0.0, None, Alu.is_equal, None, ["sm_s4"], ["sm_s4"])
        p.ts(smalls[:, 5:6], smalls[:, 4:5], -1.0, 1.0, Alu.mult, Alu.add, ["sm_s4"], ["sm_s5"])
        p.stt(smalls[:, 3:4], smalls[:, 5:6], K29, smalls[:, 4:5], Alu.mult, Alu.add,
              ["sm_s4", "sm_s5"], ["sm_K"])
        K_ap = smalls[:, 3:4]
        p.ts(smalls[:, 6:7], smalls[:, 3:4], CMAG_A, None, Alu.mult, None, ["sm_K"], ["sm_KC"])
        p.ts(smalls[:, 7:8], smalls[:, 3:4], CMAG_W, None, Alu.mult, None, ["sm_K"], ["sm_KC"])
        p.ts(smalls[:, 8:9], smalls[:, 6:7], 2.0, None, Alu.mult, None, ["sm_KC"], ["sm_KC"])
        p.ts(smalls[:, 9:10], smalls[:, 7:8], 2.0, None, Alu.mult, None, ["sm_KC"], ["sm_KC"])
        p.act(smalls[:, 0:1], smalls[:, 1:2], Act.Sigmoid, ["sm_gate"], ["sm_gsig"])
        gsig_ap = smalls[:, 0:1]
        cflag_ap = smalls[:, 2:3]

        def emit_search(amax_ap, keys_amax, q_ap, kq, qi_ap, kqi, t_ap, kt, c_ap, kc, f_ap):
            n = c_ap.shape[1]
            ka = list(keys_amax)
            p.ts(amax_ap, amax_ap, SEARCH_PRE, None, Alu.mult, None, ka, ka)
            p.memset(q_ap, 1.0, [kq])
            for b in SEARCH_STEPS:
                p.stt(c_ap, q_ap, float(2.0 ** b), amax_ap, Alu.mult, Alu.is_le,
                      [kq] + ka, [kc])
                p.stt(c_ap, c_ap, float(2.0 ** b - 1.0), q_ap, Alu.mult, Alu.mult,
                      [kc, kq], [kc])
                p.tt(q_ap, q_ap, c_ap, Alu.add, [kq, kc], [kq])
            p.ts(q_ap, q_ap, SEARCH_POST_S, None, Alu.mult, None, [kq], [kq])
            p.recip(qi_ap, q_ap, [kq], [kqi])

        QC = [0]
        QPAT = (1,)

        def flushq(tmp):
            if tmp["pend"] is not None:
                tmp["pend"]()
                tmp["pend"] = None

        def emit_quant(src_ap, src_keys, inv_bc, scale_tile_ap, blk0, scale_keys,
                       out_ap, out_keys, kc_col, hi, tmp, tasp, post=()):
            k = QC[0]
            QC[0] += 1
            se = "gp" if QPAT[k % len(QPAT)] else "dve"
            par = tmp["p" + se]
            tmp["p" + se] = 1 - par
            md = tmp["md"][se][par]
            ta, ta_keys, yb, yb_keys = tasp
            ky = tmp["ky"] + se
            kmd = ky + "d" + str(par)
            if se == "gp" and par == 1 and yb is not None:
                y, kys = yb, list(yb_keys)
            else:
                y, kys = tmp["y"][se], [ky + str(par if se == "gp" else 0)]
            W = src_ap.shape[-1] if len(src_ap.shape) == 2 else None
            assert W is not None
            yv, mdv, tav = y[:, 0:W], md[:, 0:W], ta[:, 0:W]
            p.tt(r3(yv), r3(src_ap), inv_bc, Alu.mult,
                 list(src_keys) + list(scale_keys), kys, eng=se)
            p.ts(yv, yv, hi, -hi, Alu.min, Alu.max, kys, kys, eng=se)
            p.act(mdv, yv, Act.Square, kys, [kmd])
            p.ts(tav, mdv, 4.0, 1.0, Alu.is_ge, Alu.add, [kmd], list(ta_keys))
            p.ts(mdv, mdv, 16.0, smalls[:, kc_col + 2:kc_col + 3], Alu.is_ge, Alu.mult,
                 [kmd, "sm_KC"], [kmd])
            p.stt(mdv, tav, smalls[:, kc_col:kc_col + 1], mdv, Alu.mult, Alu.add,
                  [kmd, "sm_KC"] + list(ta_keys), [kmd])
            scb = scale_tile_ap[:, blk0: blk0 + W // 32].unsqueeze(2).broadcast_to(
                [128, W // 32, 32])
            if tmp["pend"] is not None:
                tmp["pend"]()
                tmp["pend"] = None

            def tail(_yv=yv, _mdv=mdv, _kys=kys, _kmd=kmd, _scb=scb, _out=out_ap,
                     _ok=list(out_keys), _sk=list(scale_keys), _se=se, _post=tuple(post)):
                p.tt(_yv, _yv, _mdv, Alu.add, _kys + [_kmd], _kys, eng=_se)
                p.tt(_yv, _yv, _mdv, Alu.subtract, _kys + [_kmd], _kys, eng=_se)
                p.tt(r3(_out), r3(_yv), _scb, Alu.mult, _kys + _sk, _ok, eng=_se)
                for fn in _post:
                    fn()
            if SKEW:
                tmp["pend"] = tail
            else:
                tail()

        def emit_ln(tt, key, scr_ap, scr_keys):
            tile = BIG2[:, tt * H:(tt + 1) * H]
            c0 = 4 * (tt % 4)
            tt = tt % 4
            p.red(stat[:, c0:c0 + 1], tile, Alu.add, [key], [f"st{tt}a"])
            p.act(scr_ap, tile, Act.Square, [key], list(scr_keys))
            p.red(stat[:, c0 + 1:c0 + 2], scr_ap, Alu.add, list(scr_keys), [f"st{tt}b"])
            p.ts(stat[:, c0 + 2:c0 + 3], stat[:, c0:c0 + 1], -1.0 / H, None, Alu.mult, None,
                 [f"st{tt}a"], [f"st{tt}c"])
            p.tt(stat[:, c0:c0 + 1], stat[:, c0 + 2:c0 + 3], stat[:, c0 + 2:c0 + 3], Alu.mult,
                 [f"st{tt}c"], [f"st{tt}a"])
            p.ts(stat[:, c0 + 1:c0 + 2], stat[:, c0 + 1:c0 + 2], 1.0 / H, None, Alu.mult, None,
                 [f"st{tt}b"], [f"st{tt}b"])
            p.stt(stat[:, c0 + 1:c0 + 2], stat[:, c0:c0 + 1], -1.0, stat[:, c0 + 1:c0 + 2],
                  Alu.mult, Alu.add, [f"st{tt}a", f"st{tt}b"], [f"st{tt}b"])
            p.ts(stat[:, c0 + 1:c0 + 2], stat[:, c0 + 1:c0 + 2], EPS_LN, None, Alu.add, None,
                 [f"st{tt}b"], [f"st{tt}b"])
            p.recip(stat[:, c0:c0 + 1], stat[:, c0 + 1:c0 + 2], [f"st{tt}b"], [f"st{tt}a"])
            p.act(stat[:, c0 + 3:c0 + 4], stat[:, c0:c0 + 1], Act.Sqrt, [f"st{tt}a"], [f"st{tt}d"])
            p.ts(tile, tile, stat[:, c0 + 2:c0 + 3], stat[:, c0 + 3:c0 + 4], Alu.add, Alu.mult,
                 [key, f"st{tt}c", f"st{tt}d"], [key])

        with ExitStack() as stA:
            sa = lambda n, shp, dt=F32: stA.enter_context(nc.sbuf_tensor(n, shp, dt))
            pa = lambda n, shp: stA.enter_context(nc.psum_tensor(n, shp, F32))
            BIG1 = sa("BIG1", [128, HC * T])
            M_sb = sa("M_sb", [128, 128])
            umask_sb = sa("umask_sb", [128, 128])
            ident_sb = sa("ident_sb", [128, 128])
            z_sb = sa("z_sb", [128, 1])
            ones_c = sa("ones_c", [128, 1])
            ones_r = sa("ones_r", [1, 128])
            S_sb = sa("S_sb", [128, 128])
            wqb = sa("wqb", [128, H], BF16)
            wkb = sa("wkb", [128, H], BF16)
            wvb = sa("wvb", [128, H], BF16)
            wob = sa("wob", [128, H], BF16)
            p.dma("sp", "d_small2", M_sb[:], Mm, writes=["M_sb"])
            p.dma("sp", "d_small2", umask_sb[:], umask, writes=["umask_sb"])
            p.dma("sp", "d_small2", ident_sb[:], identf, writes=["ident_sb"])
            p.dma("sp", "d_small2", z_sb[:], zv, writes=["z_sb"])
            p.dma("sp", "d_small2", ones_c[:], onescol, writes=["ones_c"])
            p.dma("sp", "d_small2", ones_r[:], onesrow, writes=["ones_r"])
            with nc.sbuf_tensor("wstg", [128, H], F32) as wstg_h:
                wstg = wstg_h
                for dram, bf, key in [(wqT, wqb, "wqb"), (wkT, wkb, "wkb"), (wvT, wvb, "wvb")]:
                    p.dma("sp", "d_wat", wstg[:].rearrange("p (c d) -> p c d", d=128),
                          dram.rearrange("(c p) d -> p c d", p=128),
                          reads=["wstg"], writes=["wstg"])
                    p.cp(bf[:], wstg[:], ["wstg"], [key])
                p.dma("sp", "d_wat", wstg[:], woT, reads=["wstg"], writes=["wstg"])
                p.cp(wob[:], wstg[:], ["wstg"], ["wob"])
            xbf = sa("xbf", [128, 1024], BF16)
            pq = sa("pq", [128, T])
            pk = sa("pk", [128, T])
            vv = sa("vv", [128, T])
            pk_tm = sa("pk_tm", [128, T])
            v_tm = sa("v_tm", [128, T])
            memo = sa("memo", [128, T])
            numl = sa("numl", [128, T])
            obf = sa("obf", [128, T], BF16)
            rd_m = sa("rd_m", [1, T])
            rd_l = sa("rd_l", [1, T])
            tA = sa("tA", [128, 512])
            tB = sa("tB", [128, 512])
            tC = sa("tC", [128, 512])
            atm = sa("atm", [128, 128])
            xps = sa("xps", [128, H])
            xpb = sa("xpb", [128, H], BF16)
            zka = sa("zka", [128, 2])
            psQ = pa("psQ", [128, 512])
            psK = pa("psK", [128, 512])
            psV = pa("psV", [128, 512])
            psA = pa("psA", [128, 512])
            psB = pa("psB", [128, 512])
            psN = pa("psN", [128, 512])
            psDen = pa("psDen", [1, 512])

            for hc in range(HC):
                p.dma("sp", "d_x", BIG1[:, hc * T:(hc + 1) * T], xT[hc * 128:(hc + 1) * 128, :],
                      writes=[f"big1_{hc}"])

            p.memset(zka[:, 0:1], 0.0, ["zka"])
            xPr = xP.rearrange("(c p) t -> p c t", p=128)
            for c in range(CH):
                xs = xps[:]
                p.dma("gp", "d_xp", xs.rearrange("p (c d) -> p c d", d=128),
                      xPr[:, :, c * 128:(c + 1) * 128],
                      reads=["xps"], writes=["xps"])
                p.cp(xpb[:], xs, ["xps"], ["xpb"])
                for hc in range(HC):
                    p.mm(psK[:, 0:128], wkb[:, hc * 128:(hc + 1) * 128],
                         xpb[:, hc * 128:(hc + 1) * 128], ["wkb", "xpb"], ["psK"],
                         start=(hc == 0), stop=(hc == HC - 1), quiet=(hc != HC - 1))
                    p.mm(psV[:, 0:128], wvb[:, hc * 128:(hc + 1) * 128],
                         xpb[:, hc * 128:(hc + 1) * 128], ["wvb", "xpb"], ["psV"],
                         start=(hc == 0), stop=(hc == HC - 1), quiet=(hc != HC - 1))
                p.ts(tC[:, 0:128], psK[:, 0:128], 0.0, None, Alu.min, None, ["psK"], ["tC"])
                p.act(tC[:, 128:256], tC[:, 0:128], Act.Exp, ["tC"], ["tC2"])
                p.ts(tA[:, 0:128], psK[:, 0:128], 0.0, None, Alu.max, None, ["psK"], ["tA"])
                p.tt(tA[:, 0:128], tC[:, 128:256], tA[:, 0:128], Alu.add, ["tC2", "tA"], ["tA"])
                p.cp(tB[:, 0:128], psV[:, 0:128], ["psV"], ["tB"])
                p.red(zka[:, 1:2], tA[:, 0:128], Alu.add, ["tA"], ["zkt"])
                p.tt(zka[:, 0:1], zka[:, 0:1], zka[:, 1:2], Alu.add, ["zka", "zkt"], ["zka"])
                p.trans(psA[:, 0:128], tA[:, 0:128], ident_sb[:], ["tA", "ident_sb"], ["psA"])
                p.trans(psA[:, 128:256], tB[:, 0:128], ident_sb[:], ["tB", "ident_sb"], ["psA"])
                p.cp(tA[:, 128:256], psA[:, 0:128], ["psA"], ["tA2"])
                p.cp(tB[:, 128:256], psA[:, 128:256], ["psA"], ["tB2"])
                p.mm(psB[:, 0:128], tA[:, 128:256], tB[:, 128:256], ["tA2", "tB2"], ["psB"],
                     start=(c == 0), stop=(c == CH - 1), quiet=(c != CH - 1))
            p.ts(S_sb[:], psB[:, 0:128], cflag_ap, None, Alu.mult, None,
                 ["psB", "sm_cflag"], ["S_sb"])
            p.ts(zka[:, 0:1], zka[:, 0:1], cflag_ap, None, Alu.mult, None,
                 ["zka", "sm_cflag"], ["zka"])

            for tt in range(2):
                for hc in range(HC):
                    slot = hc % 2
                    xsl = xbf[:, slot * 512:(slot + 1) * 512]
                    p.cp(xsl, BIG1[:, hc * T + tt * 512: hc * T + tt * 512 + 512],
                         [f"big1_{hc}"], [f"xbf{slot}"])
                    p.mm(psQ[:], wqb[:, hc * 128:(hc + 1) * 128], xsl, ["wqb", f"xbf{slot}"],
                         ["psQ"], start=(hc == 0), stop=(hc == HC - 1), quiet=(hc != HC - 1))
                    p.mm(psK[:], wkb[:, hc * 128:(hc + 1) * 128], xsl, ["wkb", f"xbf{slot}"],
                         ["psK"], start=(hc == 0), stop=(hc == HC - 1), quiet=(hc != HC - 1))
                    p.mm(psV[:], wvb[:, hc * 128:(hc + 1) * 128], xsl, ["wvb", f"xbf{slot}"],
                         ["psV"], start=(hc == 0), stop=(hc == HC - 1), quiet=False)
                sl = slice(tt * 512, tt * 512 + 512)
                for src_ps, kps, dst, kd in [(psQ, "psQ", pq, "pq"), (psK, "psK", pk, "pk")]:
                    p.ts(tA[:], src_ps[:], 0.0, None, Alu.min, None, [kps], ["tA"])
                    p.act(tB[:], tA[:], Act.Exp, ["tA"], ["tB"])
                    p.ts(tC[:], src_ps[:], 0.0, None, Alu.max, None, [kps], ["tC"])
                    p.tt(dst[:, sl], tB[:], tC[:], Alu.add, ["tB", "tC"], [kd])
                p.cp(vv[:, sl], psV[:], ["psV"], ["vv"])

            for tt in range(2):
                sl = slice(tt * 512, tt * 512 + 512)
                p.mm(psA[:], M_sb[:], pq[:, sl], ["M_sb", "pq"], ["psA"])
                p.cp(memo[:, sl], psA[:], ["psA"], ["memo"])
                p.mm(psDen[:], z_sb[:], pq[:, sl], ["z_sb", "pq"], ["psDen"])
                p.ts(rd_m[:, sl], psDen[:], EPS_ATT, None, Alu.add, None, ["psDen"], ["rd_m"])
                p.recip(rd_m[:, sl], rd_m[:, sl], ["rd_m"], ["rd_m"])

            for c in range(CH):
                slc = slice(c * 128, (c + 1) * 128)
                p.trans(psA[:, 0:128], pk[:, slc], ident_sb[:], ["pk", "ident_sb"], ["psA"])
                p.cp(pk_tm[:, slc], psA[:, 0:128], ["psA"], ["pk_tm"])
                p.trans(psA[:, 128:256], vv[:, slc], ident_sb[:], ["vv", "ident_sb"], ["psA"])
                p.cp(v_tm[:, slc], psA[:, 128:256], ["psA"], ["v_tm"])

            for c in range(CH):
                slc = slice(c * 128, (c + 1) * 128)
                nsl = slice((c % 4) * 128, (c % 4) * 128 + 128)
                p.mm(psA[:, 0:128], pk[:, slc], pq[:, slc], ["pk", "pq"], ["psA"])
                p.tt(atm[:], psA[:, 0:128], umask_sb[:], Alu.mult, ["psA", "umask_sb"], ["atm"])
                p.mm(psN[:, nsl], v_tm[:, slc], atm[:], ["v_tm", "atm"], ["psN"],
                     start=True, stop=False, quiet=True)
                p.mm(psN[:, nsl], S_sb[:], pq[:, slc], ["S_sb", "pq"], ["psN"],
                     start=False, stop=True)
                p.cp(numl[:, slc], psN[:, nsl], ["psN"], ["numl"])
                p.mm(psDen[:, 0:128], ones_c[:], atm[:], ["ones_c", "atm"], ["psDen"],
                     start=True, stop=False, quiet=True)
                p.mm(psDen[:, 0:128], zka[:, 0:1], pq[:, slc], ["zka", "pq"], ["psDen"],
                     start=False, stop=True)
                p.ts(rd_l[:, slc], psDen[:, 0:128], EPS_ATT, None, Alu.add, None,
                     ["psDen"], ["rd_l"])
                p.recip(rd_l[:, slc], rd_l[:, slc], ["rd_l"], ["rd_l"])
                p.mm(psB[:, 0:128], pk_tm[:, slc], v_tm[:, slc], ["pk_tm", "v_tm"], ["psB"])
                p.tt(S_sb[:], S_sb[:], psB[:, 0:128], Alu.add, ["S_sb", "psB"], ["S_sb"])
                p.red(zka[:, 1:2], pk[:, slc], Alu.add, ["pk"], ["zkt"])
                p.tt(zka[:, 0:1], zka[:, 0:1], zka[:, 1:2], Alu.add, ["zka", "zkt"], ["zka"])

            for tt in range(2):
                sl = slice(tt * 512, tt * 512 + 512)
                p.mm(psA[:], ones_r[:], rd_m[:, sl], ["ones_r", "rd_m"], ["psA"])
                p.mm(psB[:], ones_r[:], rd_l[:, sl], ["ones_r", "rd_l"], ["psB"])
                p.tt(tA[:], memo[:, sl], psA[:], Alu.mult, ["memo", "psA"], ["tA"])
                p.tt(tB[:], numl[:, sl], psB[:], Alu.mult, ["numl", "psB"], ["tB"])
                p.tt(tC[:], tA[:], tB[:], Alu.subtract, ["tA", "tB"], ["tC"])
                p.ts(tC[:], tC[:], gsig_ap, None, Alu.mult, None, ["tC", "sm_gsig"], ["tC"])
                p.tt(obf[:, sl], tB[:], tC[:], Alu.add, ["tB", "tC"], ["obf"])
                for hc in range(HC):
                    p.mm(psQ[:], wob[:, hc * 128:(hc + 1) * 128], obf[:, sl], ["wob", "obf"],
                         ["psQ"])
                    bsl = BIG1[:, hc * T + tt * 512: hc * T + tt * 512 + 512]
                    p.tt(bsl, bsl, psQ[:], Alu.add, [f"big1_{hc}", "psQ"], [f"big1_{hc}"])

            for tt in range(NTT):
                for hc in range(HC):
                    half = hc % 2
                    psl = psA[:, half * 128:half * 128 + 128]
                    p.trans(psl, BIG1[:, hc * T + tt * 128: hc * T + tt * 128 + 128],
                            ident_sb[:], [f"big1_{hc}", "ident_sb"], ["psA"])
                    p.cp(BIG2[:, tt * H + hc * 128: tt * H + hc * 128 + 128], psl,
                         ["psA"], [f"big2_{tt}"])
            p.barrier()

        XQF = sbg("XQF", [128, HC * T], BF16)
        with ExitStack() as stC:
            pass_sc = None
            sc_ = lambda n, shp, dt=F32: stC.enter_context(nc.sbuf_tensor(n, shp, dt))
            XQ = sc_("XQ", [128, NTT * H], BF16)
            ln_scr = sc_("ln_scr", [128, H])
            amax_x = sc_("amax_x", [128, 512])
            sc_x = sc_("sc_x", [128, 512])
            inv_x = sc_("inv_x", [128, 512])
            srch_c = sc_("srch_c", [128, 512])
            tmpq = dict(y={"dve": sc_("qy0", [128, H])[:], "gp": sc_("qy1", [128, H])[:]},
                        md={"dve": [sc_("qmd0", [128, H])[:], sc_("qmd1", [128, H])[:]],
                            "gp": [sc_("qmd2", [128, H])[:], sc_("qmd3", [128, H])[:]]},
                        ky="qc", pdve=0, pgp=0, pend=None)
            qta = sc_("qta0", [128, H])[:]
            for tt in range(NTT):
                emit_ln(tt, f"big2_{tt}", ln_scr[:], ["lnscr"])
                tile = BIG2[:, tt * H:(tt + 1) * H]
                p.red(amax_x[:, tt * 64:(tt + 1) * 64].unsqueeze(2), r3(tile), Alu.max,
                      [f"big2_{tt}"], ["amax_x"], absval=True)
            emit_search(amax_x[:], ["amax_x"], sc_x[:], "sc_x", inv_x[:], "inv_x",
                        None, "sr_t", srch_c[:], "sr_c", None)
            for tt in range(NTT):
                tile = BIG2[:, tt * H:(tt + 1) * H]
                ib = inv_x[:, tt * 64:(tt + 1) * 64].unsqueeze(2).broadcast_to([128, 64, 32])
                def post_c(_tt=tt):
                    p.dma("sp", "d_xq",
                          XQF[:].rearrange("p (c t) -> p c t", t=T)[
                              :, :, _tt * 128:(_tt + 1) * 128],
                          XQ[:, _tt * H:(_tt + 1) * H], reads=[f"xq_{_tt}"],
                          writes=["xqf"], transpose=True)
                emit_quant(tile, [f"big2_{tt}"], ib, sc_x[:], tt * 64, ["inv_x", "sc_x"],
                           XQ[:, tt * H:(tt + 1) * H], [f"xq_{tt}"], 6, CLAMP_A, tmpq,
                           (qta, ["qcta"], ln_scr[:], ["lnscr"]), post=(post_c,))
            flushq(tmpq)
            p.barrier()

        with ExitStack() as stF:
            sf = lambda n, shp, dt=F32: stF.enter_context(nc.sbuf_tensor(n, shp, dt))
            pf = lambda n, shp: stF.enter_context(nc.psum_tensor(n, shp, F32))
            WGT = sf("WGT", [128, HC * ISLAB], BF16)
            WUT = sf("WUT", [128, HC * ISLAB], BF16)
            WDT = sf("WDT", [128, NQT * H], BF16)
            WST = sf("WST", [128, H * 2])
            WQN = sf("WQN", [128, H // 2], BF16)
            scw = sf("scw", [128, 256])
            invw = sf("invw", [128, 256])
            s_c2 = sf("s_c2", [128, 128])
            wpc = [0]
            igc = [0]
            PSC = pf("PSC", [128, 160])
            am_w = PSC[:, 0:128]
            am_i = PSC[:, 128:160]
            INTER = sf("INTER", [128, 4 * ISLAB])
            INTQ = sf("INTQ", [128, 2 * ISLAB], BF16)
            INTQF = sf("INTQF", [128, NQT * T], BF16)
            sci = sf("sci", [128, 64])
            invi = sf("invi", [128, 64])
            HW2 = H // 2
            tmpw = dict(y={"dve": sf("fy0", [128, HW2])[:], "gp": sf("fy1", [128, HW2])[:]},
                        md={"dve": [sf("fmd0", [128, HW2])[:], sf("fmd1", [128, HW2])[:]],
                            "gp": [sf("fmd2", [128, HW2])[:], sf("fmd3", [128, HW2])[:]]},
                        ky="wc", pdve=0, pgp=0, pend=None)
            tmpi = tmpw
            psG = pf("psG", [128, 512])
            psU = pf("psU", [128, 512])
            psYl = [pf(f"psY{i_}", [128, 512]) for i_ in range(4)]

            WPEND = [None]

            def wq_load_tile(dram, s, ti, kind):
                slot = ti % 2
                wst = WST[:, slot * H:(slot + 1) * H]
                if kind == "d":
                    srcap = wd.rearrange("(c p) i -> p c i", p=128)[
                        :, :, s * ISLAB + ti * 128: s * ISLAB + (ti + 1) * 128]
                    dstw = wst.rearrange("p (c i) -> p c i", i=128)
                    p.dma("act", f"d_ws{slot}", dstw[:, 0:8, :], srcap[:, 0:8, :],
                          reads=[f"wst{slot}a"], writes=[f"wst{slot}a"])
                    p.dma("act", f"d_ws{slot}x", dstw[:, 8:16, :], srcap[:, 8:16, :],
                          reads=[f"wst{slot}b"], writes=[f"wst{slot}b"])
                else:
                    srcap = dram[s * ISLAB + ti * 128: s * ISLAB + (ti + 1) * 128, :]
                    p.dma("act", f"d_ws{slot}", wst[:, 0:1024], srcap[:, 0:1024],
                          reads=[f"wst{slot}a"], writes=[f"wst{slot}a"])
                    p.dma("act", f"d_ws{slot}x", wst[:, 1024:2048], srcap[:, 1024:2048],
                          reads=[f"wst{slot}b"], writes=[f"wst{slot}b"])
                p.red(am_w[:, slot * 64:(slot + 1) * 64].unsqueeze(2), r3(wst), Alu.max,
                      [f"wst{slot}a", f"wst{slot}b"], [f"amw{slot}"], absval=True)

            def wq_back(job, nxt):
                dram, s, t0, kind, wkey = job
                wpar = wpc[0] % 2
                wpc[0] += 1
                scwS = scw[:, wpar * 128:(wpar + 1) * 128]
                invwS = invw[:, wpar * 128:(wpar + 1) * 128]
                skeys = [f"scw{wpar}", f"invw{wpar}"]
                emit_search(am_w[:], ["amw0", "amw1"], scwS, skeys[0], invwS, skeys[1],
                            None, "wt2", s_c2[:], "wc2", None)
                for idx, ti in enumerate((t0, t0 + 1)):
                    slot = ti % 2
                    wst = WST[:, slot * H:(slot + 1) * H]
                    for hh in range(2):
                        kst = f"wst{slot}" + ("b" if hh else "a")
                        c0 = slot * 64 + hh * 32
                        ib = invwS[:, c0:c0 + 32].unsqueeze(2).broadcast_to([128, 32, 32])
                        if kind == "d":
                            dst = WDT[:, ti * H + hh * 1024: ti * H + (hh + 1) * 1024
                                      ].rearrange("p (c j) -> p c j", j=128)
                        else:
                            buf = WGT if kind == "g" else WUT
                            dst = buf[:].rearrange("p (c i) -> p c i", i=ISLAB)[
                                :, hh * 8:(hh + 1) * 8, ti * 128:(ti + 1) * 128]

                        def post_w(_dst=dst, _sem={"g": "d_trg", "u": "d_tru",
                                                   "d": "d_trd"}[kind], _wk=wkey):
                            p.dma("sp", _sem, _dst, WQN[:], reads=["wqn"], writes=[_wk],
                                  transpose=True)
                        emit_quant(wst[:, hh * 1024:(hh + 1) * 1024], [kst], ib, scwS, c0,
                                   skeys, WQN[:], ["wqn"],
                                   7, CLAMP_W, tmpw,
                                   (INTER[:, 0:1024], ["int_0", "int_1"],
                                    INTER[:, 1024:2048], ["int_2", "int_3"]),
                                   post=(post_w,))
                    if nxt is not None:
                        wq_load_tile(nxt[0], nxt[1], nxt[2] + idx, nxt[3])

            def wq_push(job):
                if WPEND[0] is None:
                    for idx in range(2):
                        wq_load_tile(job[0], job[1], job[2] + idx, job[3])
                else:
                    wq_back(WPEND[0], job)
                WPEND[0] = job

            def wq_flush_pipe():
                if WPEND[0] is not None:
                    wq_back(WPEND[0], None)
                    WPEND[0] = None

            def down_section(s):
                flushq(tmpw)
                for tt in range(NTT):
                    for hs in range(4):
                        psy = psYl[hs]
                        kk = f"psY{hs}"
                        for ic in range(NQT):
                            p.mm(psy[:], INTQF[:, ic * T + tt * 128: ic * T + tt * 128 + 128],
                                 WDT[:, ic * H + hs * 512: ic * H + hs * 512 + 512],
                                 ["intqf", "WDT"], [kk], start=(ic == 0),
                                 stop=(ic == NQT - 1), quiet=(ic != NQT - 1))
                        dstb = BIG2[:, tt * H + hs * 512: tt * H + hs * 512 + 512]
                        p.tt(dstb, dstb, psy[:], Alu.add, [f"big2_{tt}", kk], [f"big2_{tt}"])

            for s in range(NS):
                for pair in range(NQT // 2):
                    wq_push((wg, s, pair * 2, "g", "WGT"))
                for pair in range(NQT // 2):
                    wq_push((wu, s, pair * 2, "u", "WUT"))
                wq_flush_pipe()
                if s > 0:
                    down_section(s - 1)
                for ihalf in range(2):
                    for qq in range(2):
                        flushq(tmpw)
                        for jj in range(2):
                            t4 = qq * 2 + jj
                            tt = ihalf * 4 + t4
                            psg = psG if tt % 2 == 0 else psYl[0]
                            psu = psU if tt % 2 == 0 else psYl[1]
                            kg = "psG" if tt % 2 == 0 else "psY0"
                            ku = "psU" if tt % 2 == 0 else "psY1"
                            for hc in range(HC):
                                lhs = XQF[:, hc * T + tt * 128: hc * T + tt * 128 + 128]
                                p.mm(psg[:], lhs, WGT[:, hc * ISLAB:(hc + 1) * ISLAB],
                                     ["xqf", "WGT"], [kg], start=(hc == 0),
                                     stop=(hc == HC - 1), quiet=(hc != HC - 1))
                                p.mm(psu[:], lhs, WUT[:, hc * ISLAB:(hc + 1) * ISLAB],
                                     ["xqf", "WUT"], [ku], start=(hc == 0),
                                     stop=(hc == HC - 1), quiet=(hc != HC - 1))
                            isl = INTER[:, t4 * ISLAB:(t4 + 1) * ISLAB]
                            p.act(isl, psg[:], Act.Silu, [kg], [f"int_{t4}"])
                            p.tt(isl, isl, psu[:], Alu.mult, [f"int_{t4}", ku],
                                 [f"int_{t4}"])
                            p.red(am_i[:, jj * 16:(jj + 1) * 16].unsqueeze(2), r3(isl),
                                  Alu.max, [f"int_{t4}"], [f"ami{jj}"], absval=True)
                        ipar = igc[0] % 2
                        igc[0] += 1
                        sciS = sci[:, ipar * 32:(ipar + 1) * 32]
                        inviS = invi[:, ipar * 32:(ipar + 1) * 32]
                        ikeys = [f"sci{ipar}", f"invi{ipar}"]
                        emit_search(am_i[:, 0:32], ["ami0", "ami1"], sciS,
                                    ikeys[0], inviS, ikeys[1], None, "it2",
                                    s_c2[:, 0:32], "ic2", None)
                        sl = slice(qq * 2 * ISLAB, (qq + 1) * 2 * ISLAB)
                        ib = inviS.unsqueeze(2).broadcast_to([128, 32, 32])

                        def post_i(_ihalf=ihalf, _qq=qq):
                            for j in range(2):
                                tt = _ihalf * 4 + _qq * 2 + j
                                srcq = INTQ[:, j * ISLAB:(j + 1) * ISLAB]
                                dst = INTQF[:].rearrange("p (c t) -> p c t", t=T)[
                                    :, :, tt * 128:(tt + 1) * 128]
                                p.dma("sp", "d_intq", dst, srcq, reads=["intq"],
                                      writes=["intqf"], transpose=True)
                        emit_quant(INTER[:, sl], [f"int_{qq * 2 + j}" for j in range(2)],
                                   ib, sciS, 0, ikeys, INTQ[:],
                                   ["intq"], 6, CLAMP_A, tmpi,
                                   (WST[:, 0:1024], ["wst0a"],
                                    WST[:, 1024:2048], ["wst0b"]), post=(post_i,))
                flushq(tmpw)
                for pair in range(NQT // 2):
                    wq_push((wd, s, pair * 2, "d", "WDT"))
            wq_flush_pipe()
            down_section(NS - 1)

            flushq(tmpw)
            for tt in range(NTT):
                emit_ln(tt, f"big2_{tt}", INTER[:, 0:H],
                        ["int_0", "int_1", "int_2", "int_3"])
                p.dma("gp", "d_out", yout[tt * 128:(tt + 1) * 128, :],
                      BIG2[:, tt * H:(tt + 1) * H], reads=[f"big2_{tt}"], writes=[])
            p.wait_raw("gp", "d_out", p.semv["d_out"])

            with nc.Block() as blk:
                @blk.sync
                def _(h):
                    for f in p.ops["sp"]:
                        f(h)

                @blk.tensor
                def _(h):
                    for f in p.ops["pe"]:
                        f(h)

                @blk.vector
                def _(h):
                    for f in p.ops["dve"]:
                        f(h)

                @blk.scalar
                def _(h):
                    for f in p.ops["act"]:
                        f(h)

                @blk.gpsimd
                def _(h):
                    for f in p.ops["gp"]:
                        f(h)
    return nc


_NC_CACHE = [None]


def kernel(L0_attn_hidden_states, L0_attn_w_q, L0_attn_w_k, L0_attn_w_v, L0_attn_w_o,
           L0_attn_gate, L0_attn_M, L0_attn_z, L0_ffn_w_gate, L0_ffn_w_up, L0_ffn_w_down):
    x = np.ascontiguousarray(np.asarray(L0_attn_hidden_states, dtype=np.float32))
    xT_all = np.ascontiguousarray(x.reshape(B * S, H).T)
    wq = np.ascontiguousarray(np.asarray(L0_attn_w_q, dtype=np.float32).T)
    wk = np.ascontiguousarray(np.asarray(L0_attn_w_k, dtype=np.float32).T)
    wv = np.ascontiguousarray(np.asarray(L0_attn_w_v, dtype=np.float32).T)
    wo = np.ascontiguousarray(np.asarray(L0_attn_w_o, dtype=np.float32).T)
    Mm = np.ascontiguousarray(np.asarray(L0_attn_M, dtype=np.float32))
    zz = np.ascontiguousarray(np.asarray(L0_attn_z, dtype=np.float32).reshape(D, 1))
    gate = np.ascontiguousarray(
        np.broadcast_to(np.asarray(L0_attn_gate, dtype=np.float32).reshape(1, 1), (128, 1)))
    wgA = np.ascontiguousarray(np.asarray(L0_ffn_w_gate, dtype=np.float32))
    wuA = np.ascontiguousarray(np.asarray(L0_ffn_w_up, dtype=np.float32))
    wdA = np.ascontiguousarray(np.asarray(L0_ffn_w_down, dtype=np.float32))
    umask = np.ascontiguousarray(np.triu(np.ones((128, 128), dtype=np.float32)))
    ident = np.ascontiguousarray(np.eye(128, dtype=np.float32))
    onescol = np.ones((128, 1), dtype=np.float32)
    onesrow = np.ones((1, 128), dtype=np.float32)
    zeros_xp = np.zeros((H, T), dtype=np.float32)
    cf1 = np.ones((128, 1), dtype=np.float32)
    cf0 = np.zeros((128, 1), dtype=np.float32)

    if _NC_CACHE[0] is None:
        _NC_CACHE[0] = build_program()
    nc = _NC_CACHE[0]
    in_maps = []
    for c in range(8):
        tok0 = c * T
        xTc = np.ascontiguousarray(xT_all[:, tok0:tok0 + T])
        if c % 2 == 1:
            xPc = np.ascontiguousarray(xT_all[:, tok0 - T:tok0])
            cf = cf1
        else:
            xPc = zeros_xp
            cf = cf0
        in_maps.append({
            "xT": xTc, "xP": xPc, "wqT": wq, "wkT": wk, "wvT": wv, "woT": wo,
            "Mm": Mm, "zv": zz, "gate": gate, "cflag": cf, "umask": umask,
            "identf": ident, "onescol": onescol, "onesrow": onesrow,
            "wg": wgA, "wu": wuA, "wd": wdA,
        })
    res = run_bass_kernel_spmd(nc, in_maps, core_ids=list(range(8)))
    out = np.concatenate([np.asarray(r["y"]) for r in res.results], axis=0)
    return out.reshape(B, S, H)


# version 24
# speedup vs baseline: 1.0625x; 1.0625x over previous
import numpy as np
import concourse.bass as bass
import concourse.mybir as mybir
from concourse.bass_utils import run_bass_kernel_spmd
from contextlib import ExitStack

F32 = mybir.dt.float32
BF16 = mybir.dt.bfloat16
Alu = mybir.AluOpType
Act = mybir.ActivationFunctionType
AX = mybir.AxisListType.X

B, S, H, D, I = 4, 2048, 2048, 128, 8192
T = 1024
HC = H // 128
CH = T // 128
NTT = T // 128
ISLAB = 512
NS = I // ISLAB
NQT = ISLAB // 128
EPS_ATT = 1e-6
EPS_LN = 1e-5
CMAG_A = 1572864.0
CMAG_W = 6291456.0
CLAMP_A = 7.625
CLAMP_W = 6.5
C64 = 6755399441055744.0
K29 = 536870912.0
SEARCH_PRE = 16777216.0
SEARCH_STEPS = [16, 8, 4, 2, 1]
SEARCH_POST_S = 2.0 ** (-26)
SEARCH_POST_I = 2.0 ** 26
SKEW = True


class Prog:
    ENGS = ("sp", "pe", "dve", "act", "gp")

    def __init__(self, nc, stack):
        self.nc = nc
        self.ops = {e: [] for e in self.ENGS}
        self.semh = {}
        self.semv = {}
        self.last_w = {}
        self.readers = {}
        self.marks = {e: {} for e in self.ENGS}
        self.stack = stack
        for e in self.ENGS:
            self.mksem("e_" + e)

    def mksem(self, name):
        self.semh[name] = self.stack.enter_context(self.nc.semaphore(name))
        self.semv[name] = 0
        return name

    def emit(self, eng, build, reads=(), writes=(), sem=None, inc=1, quiet=False):
        sem = sem or ("e_" + eng)
        own = "e_" + eng
        pe_skip = (eng == "pe")
        need = {}
        for k in reads:
            tok = self.last_w.get(k)
            if tok is not None and not (pe_skip and tok[0] == own):
                need[tok[0]] = max(need.get(tok[0], 0), tok[1])
        for k in writes:
            tok = self.last_w.get(k)
            if tok is not None and not (pe_skip and tok[0] == own):
                need[tok[0]] = max(need.get(tok[0], 0), tok[1])
            for s2, v2 in self.readers.get(k, {}).items():
                if pe_skip and s2 == own:
                    continue
                need[s2] = max(need.get(s2, 0), v2)
        marks = self.marks[eng]
        waits = []
        for s2 in sorted(need):
            v2 = need[s2]
            if s2.startswith("d_"):
                v2 = self.semv[s2]
            if marks.get(s2, 0) < v2:
                waits.append((self.semh[s2], v2))
                marks[s2] = v2
        if quiet:
            tok = (sem, self.semv[sem] + inc)
            hsem = None
        else:
            self.semv[sem] += inc
            tok = (sem, self.semv[sem])
            hsem = self.semh[sem]

        def go(h, _w=tuple(waits), _b=build, _hs=hsem, _inc=inc):
            for hs, v in _w:
                h.wait_ge(hs, v)
            ins = _b(h)
            if _hs is not None:
                ins.then_inc(_hs, _inc)
        self.ops[eng].append(go)
        for k in reads:
            d = self.readers.setdefault(k, {})
            d[tok[0]] = max(d.get(tok[0], 0), tok[1])
        for k in writes:
            self.last_w[k] = tok
            self.readers[k] = {}
        return tok

    def wait_raw(self, eng, sem, val):
        hs = self.semh[sem]
        self.ops[eng].append(lambda h, _hs=hs, _v=val: h.wait_ge(_hs, _v))

    def barrier(self):
        for e in self.ENGS:
            marks = self.marks[e]
            for s, v in self.semv.items():
                if v > 0 and marks.get(s, 0) < v:
                    self.wait_raw(e, s, v)
                    marks[s] = v

    def dma(self, eng, sem, out, in_, reads=(), writes=(), transpose=False):
        return self.emit(eng, lambda h: h.dma_start(out=out, in_=in_, transpose=transpose),
                         reads=reads, writes=writes, sem=sem, inc=16)

    def tt(self, out, a, b, op, reads, writes, eng="dve"):
        return self.emit(eng, lambda h: h.tensor_tensor(out, a, b, op), reads, writes)

    def ts(self, out, a, s1, s2, op1, op2, reads, writes, eng="dve"):
        if s2 is None:
            return self.emit(eng, lambda h: h.tensor_single_scalar(out, a, s1, op1), reads, writes)
        return self.emit(eng, lambda h: h.tensor_scalar(out, a, s1, s2, op1, op2), reads, writes)

    def stt(self, out, a, s, b, op1, op2, reads, writes, eng="dve"):
        return self.emit(eng, lambda h: h.scalar_tensor_tensor(out, a, s, b, op1, op2), reads, writes)

    def cp(self, out, a, reads, writes, eng="dve"):
        return self.emit(eng, lambda h: h.tensor_copy(out, a), reads, writes)

    def recip(self, out, a, reads, writes):
        return self.emit("dve", lambda h: h.reciprocal(out, a), reads, writes)

    def red(self, out, a, op, reads, writes, absval=False):
        return self.emit("dve", lambda h: h.tensor_reduce(out, a, AX, op,
                                                          apply_absolute_value=absval),
                         reads, writes)

    def act(self, out, a, func, reads, writes, bias=0.0, scale=1.0):
        return self.emit("act", lambda h: h.activation(out, a, func, bias=bias, scale=scale),
                         reads, writes)

    def mm(self, out, lhsT, rhs, reads, writes, start=True, stop=True, quiet=False):
        return self.emit("pe", lambda h: h.matmul(out, lhsT, rhs, start=start, stop=stop),
                         reads, writes, quiet=quiet)

    def trans(self, out, in_, ident, reads, writes):
        return self.emit("pe", lambda h: h.transpose(out, in_, ident), reads, writes)

    def memset(self, ap, val, writes, eng="dve"):
        return self.emit(eng, lambda h: h.memset(ap, val), (), writes)


def r3(ap, w=32):
    return ap.rearrange("p (b w) -> p b w", w=w)


def build_program():
    nc = bass.Bass("TRN2", target_bir_lowering=False)
    xT = nc.dram_tensor("xT", [H, T], F32, kind="ExternalInput").ap()
    xP = nc.dram_tensor("xP", [H, T], F32, kind="ExternalInput").ap()
    wqT = nc.dram_tensor("wqT", [H, D], F32, kind="ExternalInput").ap()
    wkT = nc.dram_tensor("wkT", [H, D], F32, kind="ExternalInput").ap()
    wvT = nc.dram_tensor("wvT", [H, D], F32, kind="ExternalInput").ap()
    woT = nc.dram_tensor("woT", [D, H], F32, kind="ExternalInput").ap()
    Mm = nc.dram_tensor("Mm", [D, D], F32, kind="ExternalInput").ap()
    zv = nc.dram_tensor("zv", [D, 1], F32, kind="ExternalInput").ap()
    gate = nc.dram_tensor("gate", [128, 1], F32, kind="ExternalInput").ap()
    cflag = nc.dram_tensor("cflag", [128, 1], F32, kind="ExternalInput").ap()
    umask = nc.dram_tensor("umask", [128, 128], F32, kind="ExternalInput").ap()
    identf = nc.dram_tensor("identf", [128, 128], F32, kind="ExternalInput").ap()
    onescol = nc.dram_tensor("onescol", [128, 1], F32, kind="ExternalInput").ap()
    onesrow = nc.dram_tensor("onesrow", [1, 128], F32, kind="ExternalInput").ap()
    wg = nc.dram_tensor("wg", [I, H], F32, kind="ExternalInput").ap()
    wu = nc.dram_tensor("wu", [I, H], F32, kind="ExternalInput").ap()
    wd = nc.dram_tensor("wd", [H, I], F32, kind="ExternalInput").ap()
    yout = nc.dram_tensor("y", [T, H], F32, kind="ExternalOutput").ap()

    with ExitStack() as stG:
        p = Prog(nc, stG)
        sbg = lambda n, shp, dt=F32: stG.enter_context(nc.sbuf_tensor(n, shp, dt))
        BIG2 = sbg("BIG2", [128, NTT * H])
        smalls = sbg("smalls", [128, 12])

        stat = sbg("stat", [128, 16])

        for s in ["d_x", "d_xp", "d_wat", "d_small", "d_small2", "d_trg", "d_tru", "d_trd", "d_out",
                  "d_ws0", "d_ws1", "d_ws0x", "d_ws1x", "d_intq", "d_xq"]:
            p.mksem(s)

        p.dma("sp", "d_small", smalls[:, 1:2], gate, writes=["sm_gate"])
        p.dma("sp", "d_small", smalls[:, 2:3], cflag, writes=["sm_cflag"])
        p.memset(smalls[:, 4:5], 1.0, ["sm_s4"])
        p.ts(smalls[:, 4:5], smalls[:, 4:5], C64, -C64, Alu.add, Alu.add, ["sm_s4"], ["sm_s4"])
        p.ts(smalls[:, 4:5], smalls[:, 4:5], 0.0, None, Alu.is_equal, None, ["sm_s4"], ["sm_s4"])
        p.ts(smalls[:, 5:6], smalls[:, 4:5], -1.0, 1.0, Alu.mult, Alu.add, ["sm_s4"], ["sm_s5"])
        p.stt(smalls[:, 3:4], smalls[:, 5:6], K29, smalls[:, 4:5], Alu.mult, Alu.add,
              ["sm_s4", "sm_s5"], ["sm_K"])
        K_ap = smalls[:, 3:4]
        p.ts(smalls[:, 6:7], smalls[:, 3:4], CMAG_A, None, Alu.mult, None, ["sm_K"], ["sm_KC"])
        p.ts(smalls[:, 7:8], smalls[:, 3:4], CMAG_W, None, Alu.mult, None, ["sm_K"], ["sm_KC"])
        p.ts(smalls[:, 8:9], smalls[:, 6:7], 2.0, None, Alu.mult, None, ["sm_KC"], ["sm_KC"])
        p.ts(smalls[:, 9:10], smalls[:, 7:8], 2.0, None, Alu.mult, None, ["sm_KC"], ["sm_KC"])
        p.act(smalls[:, 0:1], smalls[:, 1:2], Act.Sigmoid, ["sm_gate"], ["sm_gsig"])
        gsig_ap = smalls[:, 0:1]
        cflag_ap = smalls[:, 2:3]

        def emit_search(amax_ap, keys_amax, q_ap, kq, qi_ap, kqi, t_ap, kt, c_ap, kc, f_ap):
            n = c_ap.shape[1]
            ka = list(keys_amax)
            p.ts(amax_ap, amax_ap, SEARCH_PRE, None, Alu.mult, None, ka, ka)
            p.memset(q_ap, 1.0, [kq])
            for b in SEARCH_STEPS:
                p.stt(c_ap, q_ap, float(2.0 ** b), amax_ap, Alu.mult, Alu.is_le,
                      [kq] + ka, [kc])
                p.stt(c_ap, c_ap, float(2.0 ** b - 1.0), q_ap, Alu.mult, Alu.mult,
                      [kc, kq], [kc])
                p.tt(q_ap, q_ap, c_ap, Alu.add, [kq, kc], [kq])
            p.ts(q_ap, q_ap, SEARCH_POST_S, None, Alu.mult, None, [kq], [kq])
            p.recip(qi_ap, q_ap, [kq], [kqi])

        QC = [0]
        QPAT = (0, 1, 1, 1)

        def flushq(tmp):
            if tmp["pend"] is not None:
                tmp["pend"]()
                tmp["pend"] = None

        def emit_quant(src_ap, src_keys, inv_bc, scale_tile_ap, blk0, scale_keys,
                       out_ap, out_keys, kc_col, hi, tmp, tasp, post=()):
            k = QC[0]
            QC[0] += 1
            se = "gp" if QPAT[k % len(QPAT)] else "dve"
            par = tmp["p" + se]
            tmp["p" + se] = 1 - par
            md = tmp["md"][se][par]
            ta, ta_keys, yb, yb_keys = tasp
            ky = tmp["ky"] + se
            kmd = ky + "d" + str(par)
            if se == "gp" and par == 1 and yb is not None:
                y, kys = yb, list(yb_keys)
            else:
                y, kys = tmp["y"][se], [ky + str(par if se == "gp" else 0)]
            W = src_ap.shape[-1] if len(src_ap.shape) == 2 else None
            assert W is not None
            yv, mdv, tav = y[:, 0:W], md[:, 0:W], ta[:, 0:W]
            p.tt(r3(yv), r3(src_ap), inv_bc, Alu.mult,
                 list(src_keys) + list(scale_keys), kys, eng=se)
            p.ts(yv, yv, hi, -hi, Alu.min, Alu.max, kys, kys, eng=se)
            p.act(mdv, yv, Act.Square, kys, [kmd])
            p.ts(tav, mdv, 4.0, 1.0, Alu.is_ge, Alu.add, [kmd], list(ta_keys))
            p.ts(mdv, mdv, 16.0, smalls[:, kc_col + 2:kc_col + 3], Alu.is_ge, Alu.mult,
                 [kmd, "sm_KC"], [kmd])
            p.stt(mdv, tav, smalls[:, kc_col:kc_col + 1], mdv, Alu.mult, Alu.add,
                  [kmd, "sm_KC"] + list(ta_keys), [kmd])
            scb = scale_tile_ap[:, blk0: blk0 + W // 32].unsqueeze(2).broadcast_to(
                [128, W // 32, 32])
            if tmp["pend"] is not None:
                tmp["pend"]()
                tmp["pend"] = None

            def tail(_yv=yv, _mdv=mdv, _kys=kys, _kmd=kmd, _scb=scb, _out=out_ap,
                     _ok=list(out_keys), _sk=list(scale_keys), _se=se, _post=tuple(post)):
                p.tt(_yv, _yv, _mdv, Alu.add, _kys + [_kmd], _kys, eng=_se)
                p.tt(_yv, _yv, _mdv, Alu.subtract, _kys + [_kmd], _kys, eng=_se)
                p.tt(r3(_out), r3(_yv), _scb, Alu.mult, _kys + _sk, _ok, eng=_se)
                for fn in _post:
                    fn()
            if SKEW:
                tmp["pend"] = tail
            else:
                tail()

        def emit_ln(tt, key, scr_ap, scr_keys):
            tile = BIG2[:, tt * H:(tt + 1) * H]
            c0 = 4 * (tt % 4)
            tt = tt % 4
            p.red(stat[:, c0:c0 + 1], tile, Alu.add, [key], [f"st{tt}a"])
            p.act(scr_ap, tile, Act.Square, [key], list(scr_keys))
            p.red(stat[:, c0 + 1:c0 + 2], scr_ap, Alu.add, list(scr_keys), [f"st{tt}b"])
            p.ts(stat[:, c0 + 2:c0 + 3], stat[:, c0:c0 + 1], -1.0 / H, None, Alu.mult, None,
                 [f"st{tt}a"], [f"st{tt}c"])
            p.tt(stat[:, c0:c0 + 1], stat[:, c0 + 2:c0 + 3], stat[:, c0 + 2:c0 + 3], Alu.mult,
                 [f"st{tt}c"], [f"st{tt}a"])
            p.ts(stat[:, c0 + 1:c0 + 2], stat[:, c0 + 1:c0 + 2], 1.0 / H, None, Alu.mult, None,
                 [f"st{tt}b"], [f"st{tt}b"])
            p.stt(stat[:, c0 + 1:c0 + 2], stat[:, c0:c0 + 1], -1.0, stat[:, c0 + 1:c0 + 2],
                  Alu.mult, Alu.add, [f"st{tt}a", f"st{tt}b"], [f"st{tt}b"])
            p.ts(stat[:, c0 + 1:c0 + 2], stat[:, c0 + 1:c0 + 2], EPS_LN, None, Alu.add, None,
                 [f"st{tt}b"], [f"st{tt}b"])
            p.recip(stat[:, c0:c0 + 1], stat[:, c0 + 1:c0 + 2], [f"st{tt}b"], [f"st{tt}a"])
            p.act(stat[:, c0 + 3:c0 + 4], stat[:, c0:c0 + 1], Act.Sqrt, [f"st{tt}a"], [f"st{tt}d"])
            p.ts(tile, tile, stat[:, c0 + 2:c0 + 3], stat[:, c0 + 3:c0 + 4], Alu.add, Alu.mult,
                 [key, f"st{tt}c", f"st{tt}d"], [key])

        with ExitStack() as stA:
            sa = lambda n, shp, dt=F32: stA.enter_context(nc.sbuf_tensor(n, shp, dt))
            pa = lambda n, shp: stA.enter_context(nc.psum_tensor(n, shp, F32))
            BIG1 = sa("BIG1", [128, HC * T])
            M_sb = sa("M_sb", [128, 128])
            umask_sb = sa("umask_sb", [128, 128])
            ident_sb = sa("ident_sb", [128, 128])
            z_sb = sa("z_sb", [128, 1])
            ones_c = sa("ones_c", [128, 1])
            ones_r = sa("ones_r", [1, 128])
            S_sb = sa("S_sb", [128, 128])
            wqb = sa("wqb", [128, H], BF16)
            wkb = sa("wkb", [128, H], BF16)
            wvb = sa("wvb", [128, H], BF16)
            wob = sa("wob", [128, H], BF16)
            p.dma("sp", "d_small2", M_sb[:], Mm, writes=["M_sb"])
            p.dma("sp", "d_small2", umask_sb[:], umask, writes=["umask_sb"])
            p.dma("sp", "d_small2", ident_sb[:], identf, writes=["ident_sb"])
            p.dma("sp", "d_small2", z_sb[:], zv, writes=["z_sb"])
            p.dma("sp", "d_small2", ones_c[:], onescol, writes=["ones_c"])
            p.dma("sp", "d_small2", ones_r[:], onesrow, writes=["ones_r"])
            with nc.sbuf_tensor("wstg", [128, H], F32) as wstg_h:
                wstg = wstg_h
                for dram, bf, key in [(wqT, wqb, "wqb"), (wkT, wkb, "wkb"), (wvT, wvb, "wvb")]:
                    p.dma("sp", "d_wat", wstg[:].rearrange("p (c d) -> p c d", d=128),
                          dram.rearrange("(c p) d -> p c d", p=128),
                          reads=["wstg"], writes=["wstg"])
                    p.cp(bf[:], wstg[:], ["wstg"], [key])
                p.dma("sp", "d_wat", wstg[:], woT, reads=["wstg"], writes=["wstg"])
                p.cp(wob[:], wstg[:], ["wstg"], ["wob"])
            xbf = sa("xbf", [128, 1024], BF16)
            pq = sa("pq", [128, T])
            pk = sa("pk", [128, T])
            vv = sa("vv", [128, T])
            pk_tm = sa("pk_tm", [128, T])
            v_tm = sa("v_tm", [128, T])
            memo = sa("memo", [128, T])
            numl = sa("numl", [128, T])
            obf = sa("obf", [128, T], BF16)
            rd_m = sa("rd_m", [1, T])
            rd_l = sa("rd_l", [1, T])
            tA = sa("tA", [128, 512])
            tB = sa("tB", [128, 512])
            tC = sa("tC", [128, 512])
            atm = sa("atm", [128, 128])
            xps = sa("xps", [128, H])
            xpb = sa("xpb", [128, H], BF16)
            zka = sa("zka", [128, 2])
            psQ = pa("psQ", [128, 512])
            psK = pa("psK", [128, 512])
            psV = pa("psV", [128, 512])
            psA = pa("psA", [128, 512])
            psB = pa("psB", [128, 512])
            psN = pa("psN", [128, 512])
            psDen = pa("psDen", [1, 512])

            for hc in range(HC):
                p.dma("sp", "d_x", BIG1[:, hc * T:(hc + 1) * T], xT[hc * 128:(hc + 1) * 128, :],
                      writes=[f"big1_{hc}"])

            p.memset(zka[:, 0:1], 0.0, ["zka"])
            xPr = xP.rearrange("(c p) t -> p c t", p=128)
            for c in range(CH):
                xs = xps[:]
                p.dma("gp", "d_xp", xs.rearrange("p (c d) -> p c d", d=128),
                      xPr[:, :, c * 128:(c + 1) * 128],
                      reads=["xps"], writes=["xps"])
                p.cp(xpb[:], xs, ["xps"], ["xpb"])
                for hc in range(HC):
                    p.mm(psK[:, 0:128], wkb[:, hc * 128:(hc + 1) * 128],
                         xpb[:, hc * 128:(hc + 1) * 128], ["wkb", "xpb"], ["psK"],
                         start=(hc == 0), stop=(hc == HC - 1), quiet=(hc != HC - 1))
                    p.mm(psV[:, 0:128], wvb[:, hc * 128:(hc + 1) * 128],
                         xpb[:, hc * 128:(hc + 1) * 128], ["wvb", "xpb"], ["psV"],
                         start=(hc == 0), stop=(hc == HC - 1), quiet=(hc != HC - 1))
                p.ts(tC[:, 0:128], psK[:, 0:128], 0.0, None, Alu.min, None, ["psK"], ["tC"])
                p.act(tC[:, 128:256], tC[:, 0:128], Act.Exp, ["tC"], ["tC2"])
                p.ts(tA[:, 0:128], psK[:, 0:128], 0.0, None, Alu.max, None, ["psK"], ["tA"])
                p.tt(tA[:, 0:128], tC[:, 128:256], tA[:, 0:128], Alu.add, ["tC2", "tA"], ["tA"])
                p.cp(tB[:, 0:128], psV[:, 0:128], ["psV"], ["tB"])
                p.red(zka[:, 1:2], tA[:, 0:128], Alu.add, ["tA"], ["zkt"])
                p.tt(zka[:, 0:1], zka[:, 0:1], zka[:, 1:2], Alu.add, ["zka", "zkt"], ["zka"])
                p.trans(psA[:, 0:128], tA[:, 0:128], ident_sb[:], ["tA", "ident_sb"], ["psA"])
                p.trans(psA[:, 128:256], tB[:, 0:128], ident_sb[:], ["tB", "ident_sb"], ["psA"])
                p.cp(tA[:, 128:256], psA[:, 0:128], ["psA"], ["tA2"])
                p.cp(tB[:, 128:256], psA[:, 128:256], ["psA"], ["tB2"])
                p.mm(psB[:, 0:128], tA[:, 128:256], tB[:, 128:256], ["tA2", "tB2"], ["psB"],
                     start=(c == 0), stop=(c == CH - 1), quiet=(c != CH - 1))
            p.ts(S_sb[:], psB[:, 0:128], cflag_ap, None, Alu.mult, None,
                 ["psB", "sm_cflag"], ["S_sb"])
            p.ts(zka[:, 0:1], zka[:, 0:1], cflag_ap, None, Alu.mult, None,
                 ["zka", "sm_cflag"], ["zka"])

            for tt in range(2):
                for hc in range(HC):
                    slot = hc % 2
                    xsl = xbf[:, slot * 512:(slot + 1) * 512]
                    p.cp(xsl, BIG1[:, hc * T + tt * 512: hc * T + tt * 512 + 512],
                         [f"big1_{hc}"], [f"xbf{slot}"])
                    p.mm(psQ[:], wqb[:, hc * 128:(hc + 1) * 128], xsl, ["wqb", f"xbf{slot}"],
                         ["psQ"], start=(hc == 0), stop=(hc == HC - 1), quiet=(hc != HC - 1))
                    p.mm(psK[:], wkb[:, hc * 128:(hc + 1) * 128], xsl, ["wkb", f"xbf{slot}"],
                         ["psK"], start=(hc == 0), stop=(hc == HC - 1), quiet=(hc != HC - 1))
                    p.mm(psV[:], wvb[:, hc * 128:(hc + 1) * 128], xsl, ["wvb", f"xbf{slot}"],
                         ["psV"], start=(hc == 0), stop=(hc == HC - 1), quiet=False)
                sl = slice(tt * 512, tt * 512 + 512)
                for src_ps, kps, dst, kd in [(psQ, "psQ", pq, "pq"), (psK, "psK", pk, "pk")]:
                    p.ts(tA[:], src_ps[:], 0.0, None, Alu.min, None, [kps], ["tA"])
                    p.act(tB[:], tA[:], Act.Exp, ["tA"], ["tB"])
                    p.ts(tC[:], src_ps[:], 0.0, None, Alu.max, None, [kps], ["tC"])
                    p.tt(dst[:, sl], tB[:], tC[:], Alu.add, ["tB", "tC"], [kd])
                p.cp(vv[:, sl], psV[:], ["psV"], ["vv"])

            for tt in range(2):
                sl = slice(tt * 512, tt * 512 + 512)
                p.mm(psA[:], M_sb[:], pq[:, sl], ["M_sb", "pq"], ["psA"])
                p.cp(memo[:, sl], psA[:], ["psA"], ["memo"])
                p.mm(psDen[:], z_sb[:], pq[:, sl], ["z_sb", "pq"], ["psDen"])
                p.ts(rd_m[:, sl], psDen[:], EPS_ATT, None, Alu.add, None, ["psDen"], ["rd_m"])
                p.recip(rd_m[:, sl], rd_m[:, sl], ["rd_m"], ["rd_m"])

            for c in range(CH):
                slc = slice(c * 128, (c + 1) * 128)
                p.trans(psA[:, 0:128], pk[:, slc], ident_sb[:], ["pk", "ident_sb"], ["psA"])
                p.cp(pk_tm[:, slc], psA[:, 0:128], ["psA"], ["pk_tm"])
                p.trans(psA[:, 128:256], vv[:, slc], ident_sb[:], ["vv", "ident_sb"], ["psA"])
                p.cp(v_tm[:, slc], psA[:, 128:256], ["psA"], ["v_tm"])

            for c in range(CH):
                slc = slice(c * 128, (c + 1) * 128)
                nsl = slice((c % 4) * 128, (c % 4) * 128 + 128)
                p.mm(psA[:, 0:128], pk[:, slc], pq[:, slc], ["pk", "pq"], ["psA"])
                p.tt(atm[:], psA[:, 0:128], umask_sb[:], Alu.mult, ["psA", "umask_sb"], ["atm"])
                p.mm(psN[:, nsl], v_tm[:, slc], atm[:], ["v_tm", "atm"], ["psN"],
                     start=True, stop=False, quiet=True)
                p.mm(psN[:, nsl], S_sb[:], pq[:, slc], ["S_sb", "pq"], ["psN"],
                     start=False, stop=True)
                p.cp(numl[:, slc], psN[:, nsl], ["psN"], ["numl"])
                p.mm(psDen[:, 0:128], ones_c[:], atm[:], ["ones_c", "atm"], ["psDen"],
                     start=True, stop=False, quiet=True)
                p.mm(psDen[:, 0:128], zka[:, 0:1], pq[:, slc], ["zka", "pq"], ["psDen"],
                     start=False, stop=True)
                p.ts(rd_l[:, slc], psDen[:, 0:128], EPS_ATT, None, Alu.add, None,
                     ["psDen"], ["rd_l"])
                p.recip(rd_l[:, slc], rd_l[:, slc], ["rd_l"], ["rd_l"])
                p.mm(psB[:, 0:128], pk_tm[:, slc], v_tm[:, slc], ["pk_tm", "v_tm"], ["psB"])
                p.tt(S_sb[:], S_sb[:], psB[:, 0:128], Alu.add, ["S_sb", "psB"], ["S_sb"])
                p.red(zka[:, 1:2], pk[:, slc], Alu.add, ["pk"], ["zkt"])
                p.tt(zka[:, 0:1], zka[:, 0:1], zka[:, 1:2], Alu.add, ["zka", "zkt"], ["zka"])

            for tt in range(2):
                sl = slice(tt * 512, tt * 512 + 512)
                p.mm(psA[:], ones_r[:], rd_m[:, sl], ["ones_r", "rd_m"], ["psA"])
                p.mm(psB[:], ones_r[:], rd_l[:, sl], ["ones_r", "rd_l"], ["psB"])
                p.tt(tA[:], memo[:, sl], psA[:], Alu.mult, ["memo", "psA"], ["tA"])
                p.tt(tB[:], numl[:, sl], psB[:], Alu.mult, ["numl", "psB"], ["tB"])
                p.tt(tC[:], tA[:], tB[:], Alu.subtract, ["tA", "tB"], ["tC"])
                p.ts(tC[:], tC[:], gsig_ap, None, Alu.mult, None, ["tC", "sm_gsig"], ["tC"])
                p.tt(obf[:, sl], tB[:], tC[:], Alu.add, ["tB", "tC"], ["obf"])
                for hc in range(HC):
                    p.mm(psQ[:], wob[:, hc * 128:(hc + 1) * 128], obf[:, sl], ["wob", "obf"],
                         ["psQ"])
                    bsl = BIG1[:, hc * T + tt * 512: hc * T + tt * 512 + 512]
                    p.tt(bsl, bsl, psQ[:], Alu.add, [f"big1_{hc}", "psQ"], [f"big1_{hc}"])

            for tt in range(NTT):
                for hc in range(HC):
                    half = hc % 2
                    psl = psA[:, half * 128:half * 128 + 128]
                    p.trans(psl, BIG1[:, hc * T + tt * 128: hc * T + tt * 128 + 128],
                            ident_sb[:], [f"big1_{hc}", "ident_sb"], ["psA"])
                    p.cp(BIG2[:, tt * H + hc * 128: tt * H + hc * 128 + 128], psl,
                         ["psA"], [f"big2_{tt}"])
            p.barrier()

        XQF = sbg("XQF", [128, HC * T], BF16)
        with ExitStack() as stC:
            pass_sc = None
            sc_ = lambda n, shp, dt=F32: stC.enter_context(nc.sbuf_tensor(n, shp, dt))
            XQ = sc_("XQ", [128, NTT * H], BF16)
            ln_scr = sc_("ln_scr", [128, H])
            amax_x = sc_("amax_x", [128, 512])
            sc_x = sc_("sc_x", [128, 512])
            inv_x = sc_("inv_x", [128, 512])
            srch_c = sc_("srch_c", [128, 512])
            tmpq = dict(y={"dve": sc_("qy0", [128, H])[:], "gp": sc_("qy1", [128, H])[:]},
                        md={"dve": [sc_("qmd0", [128, H])[:], sc_("qmd1", [128, H])[:]],
                            "gp": [sc_("qmd2", [128, H])[:], sc_("qmd3", [128, H])[:]]},
                        ky="qc", pdve=0, pgp=0, pend=None)
            qta = sc_("qta0", [128, H])[:]
            for tt in range(NTT):
                emit_ln(tt, f"big2_{tt}", ln_scr[:], ["lnscr"])
                tile = BIG2[:, tt * H:(tt + 1) * H]
                p.red(amax_x[:, tt * 64:(tt + 1) * 64].unsqueeze(2), r3(tile), Alu.max,
                      [f"big2_{tt}"], ["amax_x"], absval=True)
            emit_search(amax_x[:], ["amax_x"], sc_x[:], "sc_x", inv_x[:], "inv_x",
                        None, "sr_t", srch_c[:], "sr_c", None)
            for tt in range(NTT):
                tile = BIG2[:, tt * H:(tt + 1) * H]
                ib = inv_x[:, tt * 64:(tt + 1) * 64].unsqueeze(2).broadcast_to([128, 64, 32])
                def post_c(_tt=tt):
                    p.dma("sp", "d_xq",
                          XQF[:].rearrange("p (c t) -> p c t", t=T)[
                              :, :, _tt * 128:(_tt + 1) * 128],
                          XQ[:, _tt * H:(_tt + 1) * H], reads=[f"xq_{_tt}"],
                          writes=["xqf"], transpose=True)
                emit_quant(tile, [f"big2_{tt}"], ib, sc_x[:], tt * 64, ["inv_x", "sc_x"],
                           XQ[:, tt * H:(tt + 1) * H], [f"xq_{tt}"], 6, CLAMP_A, tmpq,
                           (qta, ["qcta"], ln_scr[:], ["lnscr"]), post=(post_c,))
            flushq(tmpq)
            p.barrier()

        with ExitStack() as stF:
            sf = lambda n, shp, dt=F32: stF.enter_context(nc.sbuf_tensor(n, shp, dt))
            pf = lambda n, shp: stF.enter_context(nc.psum_tensor(n, shp, F32))
            WGT = sf("WGT", [128, HC * ISLAB], BF16)
            WUT = sf("WUT", [128, HC * ISLAB], BF16)
            WDT = sf("WDT", [128, NQT * H], BF16)
            WST = sf("WST", [128, H * 2])
            WQN = sf("WQN", [128, H // 2], BF16)
            scw = sf("scw", [128, 256])
            invw = sf("invw", [128, 256])
            s_c2 = sf("s_c2", [128, 128])
            wpc = [0]
            igc = [0]
            PSC = pf("PSC", [128, 160])
            am_w = PSC[:, 0:128]
            am_i = PSC[:, 128:160]
            INTER = sf("INTER", [128, 4 * ISLAB])
            INTQ = sf("INTQ", [128, 2 * ISLAB], BF16)
            INTQF = sf("INTQF", [128, NQT * T], BF16)
            sci = sf("sci", [128, 64])
            invi = sf("invi", [128, 64])
            HW2 = H // 2
            tmpw = dict(y={"dve": sf("fy0", [128, HW2])[:], "gp": sf("fy1", [128, HW2])[:]},
                        md={"dve": [sf("fmd0", [128, HW2])[:], sf("fmd1", [128, HW2])[:]],
                            "gp": [sf("fmd2", [128, HW2])[:], sf("fmd3", [128, HW2])[:]]},
                        ky="wc", pdve=0, pgp=0, pend=None)
            tmpi = tmpw
            psG = pf("psG", [128, 512])
            psU = pf("psU", [128, 512])
            psYl = [pf(f"psY{i_}", [128, 512]) for i_ in range(4)]

            WPEND = [None]

            def wq_load_tile(dram, s, ti, kind):
                slot = ti % 2
                wst = WST[:, slot * H:(slot + 1) * H]
                if kind == "d":
                    srcap = wd.rearrange("(c p) i -> p c i", p=128)[
                        :, :, s * ISLAB + ti * 128: s * ISLAB + (ti + 1) * 128]
                    dstw = wst.rearrange("p (c i) -> p c i", i=128)
                    p.dma("act", f"d_ws{slot}", dstw[:, 0:8, :], srcap[:, 0:8, :],
                          reads=[f"wst{slot}a"], writes=[f"wst{slot}a"])
                    p.dma("act", f"d_ws{slot}x", dstw[:, 8:16, :], srcap[:, 8:16, :],
                          reads=[f"wst{slot}b"], writes=[f"wst{slot}b"])
                else:
                    srcap = dram[s * ISLAB + ti * 128: s * ISLAB + (ti + 1) * 128, :]
                    p.dma("act", f"d_ws{slot}", wst[:, 0:1024], srcap[:, 0:1024],
                          reads=[f"wst{slot}a"], writes=[f"wst{slot}a"])
                    p.dma("act", f"d_ws{slot}x", wst[:, 1024:2048], srcap[:, 1024:2048],
                          reads=[f"wst{slot}b"], writes=[f"wst{slot}b"])
                p.red(am_w[:, slot * 64:(slot + 1) * 64].unsqueeze(2), r3(wst), Alu.max,
                      [f"wst{slot}a", f"wst{slot}b"], [f"amw{slot}"], absval=True)

            def wq_back(job, nxt):
                dram, s, t0, kind, wkey = job
                wpar = wpc[0] % 2
                wpc[0] += 1
                scwS = scw[:, wpar * 128:(wpar + 1) * 128]
                invwS = invw[:, wpar * 128:(wpar + 1) * 128]
                skeys = [f"scw{wpar}", f"invw{wpar}"]
                emit_search(am_w[:], ["amw0", "amw1"], scwS, skeys[0], invwS, skeys[1],
                            None, "wt2", s_c2[:], "wc2", None)
                for idx, ti in enumerate((t0, t0 + 1)):
                    slot = ti % 2
                    wst = WST[:, slot * H:(slot + 1) * H]
                    for hh in range(2):
                        kst = f"wst{slot}" + ("b" if hh else "a")
                        c0 = slot * 64 + hh * 32
                        ib = invwS[:, c0:c0 + 32].unsqueeze(2).broadcast_to([128, 32, 32])
                        if kind == "d":
                            dst = WDT[:, ti * H + hh * 1024: ti * H + (hh + 1) * 1024
                                      ].rearrange("p (c j) -> p c j", j=128)
                        else:
                            buf = WGT if kind == "g" else WUT
                            dst = buf[:].rearrange("p (c i) -> p c i", i=ISLAB)[
                                :, hh * 8:(hh + 1) * 8, ti * 128:(ti + 1) * 128]

                        def post_w(_dst=dst, _sem={"g": "d_trg", "u": "d_tru",
                                                   "d": "d_trd"}[kind], _wk=wkey):
                            p.dma("sp", _sem, _dst, WQN[:], reads=["wqn"], writes=[_wk],
                                  transpose=True)
                        emit_quant(wst[:, hh * 1024:(hh + 1) * 1024], [kst], ib, scwS, c0,
                                   skeys, WQN[:], ["wqn"],
                                   7, CLAMP_W, tmpw,
                                   (INTER[:, 0:1024], ["int_0", "int_1"],
                                    INTER[:, 1024:2048], ["int_2", "int_3"]),
                                   post=(post_w,))
                    if nxt is not None:
                        wq_load_tile(nxt[0], nxt[1], nxt[2] + idx, nxt[3])

            def wq_push(job):
                if WPEND[0] is None:
                    for idx in range(2):
                        wq_load_tile(job[0], job[1], job[2] + idx, job[3])
                else:
                    wq_back(WPEND[0], job)
                WPEND[0] = job

            def wq_flush_pipe():
                if WPEND[0] is not None:
                    wq_back(WPEND[0], None)
                    WPEND[0] = None

            def down_section(s):
                flushq(tmpw)
                for tt in range(NTT):
                    for hs in range(4):
                        psy = psYl[hs]
                        kk = f"psY{hs}"
                        for ic in range(NQT):
                            p.mm(psy[:], INTQF[:, ic * T + tt * 128: ic * T + tt * 128 + 128],
                                 WDT[:, ic * H + hs * 512: ic * H + hs * 512 + 512],
                                 ["intqf", "WDT"], [kk], start=(ic == 0),
                                 stop=(ic == NQT - 1), quiet=(ic != NQT - 1))
                        dstb = BIG2[:, tt * H + hs * 512: tt * H + hs * 512 + 512]
                        p.tt(dstb, dstb, psy[:], Alu.add, [f"big2_{tt}", kk], [f"big2_{tt}"])

            for s in range(NS):
                for pair in range(NQT // 2):
                    wq_push((wg, s, pair * 2, "g", "WGT"))
                for pair in range(NQT // 2):
                    wq_push((wu, s, pair * 2, "u", "WUT"))
                wq_flush_pipe()
                if s > 0:
                    down_section(s - 1)
                for ihalf in range(2):
                    for qq in range(2):
                        flushq(tmpw)
                        for jj in range(2):
                            t4 = qq * 2 + jj
                            tt = ihalf * 4 + t4
                            psg = psG if tt % 2 == 0 else psYl[0]
                            psu = psU if tt % 2 == 0 else psYl[1]
                            kg = "psG" if tt % 2 == 0 else "psY0"
                            ku = "psU" if tt % 2 == 0 else "psY1"
                            for hc in range(HC):
                                lhs = XQF[:, hc * T + tt * 128: hc * T + tt * 128 + 128]
                                p.mm(psg[:], lhs, WGT[:, hc * ISLAB:(hc + 1) * ISLAB],
                                     ["xqf", "WGT"], [kg], start=(hc == 0),
                                     stop=(hc == HC - 1), quiet=(hc != HC - 1))
                                p.mm(psu[:], lhs, WUT[:, hc * ISLAB:(hc + 1) * ISLAB],
                                     ["xqf", "WUT"], [ku], start=(hc == 0),
                                     stop=(hc == HC - 1), quiet=(hc != HC - 1))
                            isl = INTER[:, t4 * ISLAB:(t4 + 1) * ISLAB]
                            p.act(isl, psg[:], Act.Silu, [kg], [f"int_{t4}"])
                            p.tt(isl, isl, psu[:], Alu.mult, [f"int_{t4}", ku],
                                 [f"int_{t4}"])
                            p.red(am_i[:, jj * 16:(jj + 1) * 16].unsqueeze(2), r3(isl),
                                  Alu.max, [f"int_{t4}"], [f"ami{jj}"], absval=True)
                        ipar = igc[0] % 2
                        igc[0] += 1
                        sciS = sci[:, ipar * 32:(ipar + 1) * 32]
                        inviS = invi[:, ipar * 32:(ipar + 1) * 32]
                        ikeys = [f"sci{ipar}", f"invi{ipar}"]
                        emit_search(am_i[:, 0:32], ["ami0", "ami1"], sciS,
                                    ikeys[0], inviS, ikeys[1], None, "it2",
                                    s_c2[:, 0:32], "ic2", None)
                        sl = slice(qq * 2 * ISLAB, (qq + 1) * 2 * ISLAB)
                        ib = inviS.unsqueeze(2).broadcast_to([128, 32, 32])

                        def post_i(_ihalf=ihalf, _qq=qq):
                            for j in range(2):
                                tt = _ihalf * 4 + _qq * 2 + j
                                srcq = INTQ[:, j * ISLAB:(j + 1) * ISLAB]
                                dst = INTQF[:].rearrange("p (c t) -> p c t", t=T)[
                                    :, :, tt * 128:(tt + 1) * 128]
                                p.dma("sp", "d_intq", dst, srcq, reads=["intq"],
                                      writes=["intqf"], transpose=True)
                        emit_quant(INTER[:, sl], [f"int_{qq * 2 + j}" for j in range(2)],
                                   ib, sciS, 0, ikeys, INTQ[:],
                                   ["intq"], 6, CLAMP_A, tmpi,
                                   (WST[:, 0:1024], ["wst0a"],
                                    WST[:, 1024:2048], ["wst0b"]), post=(post_i,))
                flushq(tmpw)
                for pair in range(NQT // 2):
                    wq_push((wd, s, pair * 2, "d", "WDT"))
            wq_flush_pipe()
            down_section(NS - 1)

            flushq(tmpw)
            for tt in range(NTT):
                emit_ln(tt, f"big2_{tt}", INTER[:, 0:H],
                        ["int_0", "int_1", "int_2", "int_3"])
                p.dma("gp", "d_out", yout[tt * 128:(tt + 1) * 128, :],
                      BIG2[:, tt * H:(tt + 1) * H], reads=[f"big2_{tt}"], writes=[])
            p.wait_raw("gp", "d_out", p.semv["d_out"])

            with nc.Block() as blk:
                @blk.sync
                def _(h):
                    for f in p.ops["sp"]:
                        f(h)

                @blk.tensor
                def _(h):
                    for f in p.ops["pe"]:
                        f(h)

                @blk.vector
                def _(h):
                    for f in p.ops["dve"]:
                        f(h)

                @blk.scalar
                def _(h):
                    for f in p.ops["act"]:
                        f(h)

                @blk.gpsimd
                def _(h):
                    for f in p.ops["gp"]:
                        f(h)
    return nc


_NC_CACHE = [None]


def kernel(L0_attn_hidden_states, L0_attn_w_q, L0_attn_w_k, L0_attn_w_v, L0_attn_w_o,
           L0_attn_gate, L0_attn_M, L0_attn_z, L0_ffn_w_gate, L0_ffn_w_up, L0_ffn_w_down):
    x = np.ascontiguousarray(np.asarray(L0_attn_hidden_states, dtype=np.float32))
    xT_all = np.ascontiguousarray(x.reshape(B * S, H).T)
    wq = np.ascontiguousarray(np.asarray(L0_attn_w_q, dtype=np.float32).T)
    wk = np.ascontiguousarray(np.asarray(L0_attn_w_k, dtype=np.float32).T)
    wv = np.ascontiguousarray(np.asarray(L0_attn_w_v, dtype=np.float32).T)
    wo = np.ascontiguousarray(np.asarray(L0_attn_w_o, dtype=np.float32).T)
    Mm = np.ascontiguousarray(np.asarray(L0_attn_M, dtype=np.float32))
    zz = np.ascontiguousarray(np.asarray(L0_attn_z, dtype=np.float32).reshape(D, 1))
    gate = np.ascontiguousarray(
        np.broadcast_to(np.asarray(L0_attn_gate, dtype=np.float32).reshape(1, 1), (128, 1)))
    wgA = np.ascontiguousarray(np.asarray(L0_ffn_w_gate, dtype=np.float32))
    wuA = np.ascontiguousarray(np.asarray(L0_ffn_w_up, dtype=np.float32))
    wdA = np.ascontiguousarray(np.asarray(L0_ffn_w_down, dtype=np.float32))
    umask = np.ascontiguousarray(np.triu(np.ones((128, 128), dtype=np.float32)))
    ident = np.ascontiguousarray(np.eye(128, dtype=np.float32))
    onescol = np.ones((128, 1), dtype=np.float32)
    onesrow = np.ones((1, 128), dtype=np.float32)
    zeros_xp = np.zeros((H, T), dtype=np.float32)
    cf1 = np.ones((128, 1), dtype=np.float32)
    cf0 = np.zeros((128, 1), dtype=np.float32)

    if _NC_CACHE[0] is None:
        _NC_CACHE[0] = build_program()
    nc = _NC_CACHE[0]
    in_maps = []
    for c in range(8):
        tok0 = c * T
        xTc = np.ascontiguousarray(xT_all[:, tok0:tok0 + T])
        if c % 2 == 1:
            xPc = np.ascontiguousarray(xT_all[:, tok0 - T:tok0])
            cf = cf1
        else:
            xPc = zeros_xp
            cf = cf0
        in_maps.append({
            "xT": xTc, "xP": xPc, "wqT": wq, "wkT": wk, "wvT": wv, "woT": wo,
            "Mm": Mm, "zv": zz, "gate": gate, "cflag": cf, "umask": umask,
            "identf": ident, "onescol": onescol, "onesrow": onesrow,
            "wg": wgA, "wu": wuA, "wd": wdA,
        })
    res = run_bass_kernel_spmd(nc, in_maps, core_ids=list(range(8)))
    out = np.concatenate([np.asarray(r["y"]) for r in res.results], axis=0)
    return out.reshape(B, S, H)
